# version 15 (final)
# baseline (speedup 1.0000x reference)
import numpy as np
from contextlib import ExitStack

import concourse.bass as bass
import concourse.mybir as mybir
from concourse.bass_utils import run_bass_kernel_spmd

F32 = mybir.dt.float32
BF16 = mybir.dt.bfloat16
I32 = mybir.dt.int32
AF = mybir.ActivationFunctionType
ALU = mybir.AluOpType
AX = mybir.AxisListType

B, S, D = 2, 4096, 2048
NCORE = 8
T = 1024
TT = 8
DC = 16
NH = 16
HD = 128
HID = 5632
HC = 44
KC = 32
SCALE = 1.0 / float(np.sqrt(HD))
EXPB = -20.0
EPS = 1e-5

ENGINES = ("sync", "gpsimd", "tensor", "scalar", "vector")


def build_nc():
    nc = bass.Bass("TRN2", target_bir_lowering=False, debug=False)

    x_d = nc.dram_tensor("x", [T, D], F32, kind="ExternalInput").ap()
    wqkv_d = nc.dram_tensor("wqkv", [48, DC, 128, 128], F32, kind="ExternalInput").ap()
    wout_d = nc.dram_tensor("wo", [16, 16, 128, 128], F32, kind="ExternalInput").ap()
    bqkv_d = nc.dram_tensor("bqkvr", [48 * 128], F32, kind="ExternalInput").ap()
    bout_d = nc.dram_tensor("bo", [D], F32, kind="ExternalInput").ap()
    wg_d = nc.dram_tensor("wg", [HC, DC, 128, 128], I32, kind="ExternalInput").ap()
    wu_d = nc.dram_tensor("wu", [HC, DC, 128, 128], I32, kind="ExternalInput").ap()
    wd_d = nc.dram_tensor("wd", [16, HC, 128, 128], I32, kind="ExternalInput").ap()
    sg_d = nc.dram_tensor("sg", [HID], F32, kind="ExternalInput").ap()
    su_d = nc.dram_tensor("su", [HID], F32, kind="ExternalInput").ap()
    sd_d = nc.dram_tensor("sd", [D], F32, kind="ExternalInput").ap()
    out_d = nc.dram_tensor("out", [T, D], F32, kind="ExternalOutput").ap()

    cc_in = nc.dram_tensor("cc_in", [4096, T], BF16)
    cc_out = nc.dram_tensor("cc_out", [4 * 4096, T], BF16)
    x2sp = nc.dram_tensor("x2sp", [T, D], F32)

    ops = {e: [] for e in ENGINES}
    cnt = {}

    def E(eng, fn, waits=(), inc=None, amt=1):
        ops[eng].append((list(waits), fn, inc, amt))
        if inc is not None:
            cnt[inc] = cnt.get(inc, 0) + amt
            return cnt[inc]
        return None

    def V(sem):
        return cnt.get(sem, 0)

    with ExitStack() as stack:
        def sb(name, shape, dt):
            return stack.enter_context(nc.sbuf_tensor(name, shape, dt))

        def psum(name, shape, dt):
            return stack.enter_context(nc.psum_tensor(name, shape, dt))

        arenaA = sb("arenaA", [128, 49152], BF16)
        xnf = sb("xnf", [128, 16384], BF16)
        stg = [sb(f"stg{i}", [128, 2048], I32) for i in range(2)]
        wstg = [sb(f"wstg{i}", [128, 2048], F32) for i in range(2)]
        x_stage = wstg[0]
        wsb = [sb(f"wsb{i}", [128, 2048], BF16) for i in range(4)]
        xn_tm = sb("xn_tm", [128, 2048], F32)
        abuf = [sb(f"abuf{i}", [128, 1024], BF16) for i in range(2)]
        recip = sb("recip", [128, 1024], F32)
        kst = [sb("kst0", [128, 1024], BF16)]
        kst = [kst[0], kst[0]]
        v2p = [sb(f"v2p{i}", [128, 128], BF16) for i in range(2)]
        brow = [sb(f"brow{i}", [1, 128], F32) for i in range(2)]
        spc = sb("spc", [128, 256], BF16)
        sgrows = sb("sgrows", [64, 128], F32)
        sgrows2 = sb("sgrows2", [64, 128], F32)
        utmp = sb("utmp", [128, 1024], BF16)
        sgt = sb("sgt", [128, 1024], BF16)
        ones_bf = sb("ones_bf", [128, 128], BF16)
        ident_f = sb("ident_f", [128, 128], F32)

        ones_row = sb("ones_row", [1, 512], F32)
        one1 = sb("one1", [1, 1], F32)
        sgc = sb("sgc", [128, HC], F32)
        suc = sb("suc", [128, HC], F32)
        sdc = sb("sdc", [128, 16], F32)
        boc = sb("boc", [128, 16], F32)
        stats = [sb(f"stats{i}", [128, 8], F32) for i in range(2)]
        eps_col = sb("eps_col", [128, 1], F32)
        expb_col = sb("expb_col", [128, 1], F32)
        x2p = [sb(f"x2p{i}", [128, 128], F32) for i in range(4)]

        psA = psum("psA", [128, 1024], F32)
        psB = psum("psB", [128, 1024], F32)
        psC = psum("psC", [128, 1024], F32)
        psD = psum("psD", [128, 1024], F32)
        psAB = [psA, psB]
        psCD = [psC, psD]

        sems = {}
        for sname in ("s_x", "s_br0", "s_br1", "s_kv0", "s_kv1",
                      "s_wg0", "s_wg1", "s_cc", "s_sg",
                      "s_sp0", "s_sp1", "s_sp2", "s_sp3",
                      "s_out0", "s_out1", "s_out2", "s_out3",
                      "s_xs0", "s_xs1", "s_kst", "s_v2p0", "s_v2p1",
                      "s_t", "s_s", "s_v", "s_p"):
            sems[sname] = stack.enter_context(nc.semaphore(sname))

        block = stack.enter_context(nc.Block())

        QCOL, OCOL, KCOL, VCOL = 0, 16384, 32768, 40960

        def Q_ap(h, c0, c1):
            return arenaA[:, QCOL + h * 1024 + c0:QCOL + h * 1024 + c1]

        def O_ap(h, c0, c1):
            return arenaA[:, OCOL + h * 1024 + c0:OCOL + h * 1024 + c1]

        def K_ap(hp, c0, c1):
            return arenaA[:, KCOL + hp * 4096 + c0:KCOL + hp * 4096 + c1]

        def V_ap(hp, c0, c1):
            return arenaA[:, VCOL + hp * 4096 + c0:VCOL + hp * 4096 + c1]

        def P_ap(hc, c0, c1):
            return arenaA[:, hc * 1024 + c0:hc * 1024 + c1]

        def xnf_ap(dc, c0, c1):
            return xnf[:, dc * 1024 + c0:dc * 1024 + c1]

        E("gpsimd", lambda e: e.iota(recip[:, 0:128], pattern=[[1, 128]], base=0,
                                     channel_multiplier=-1,
                                     allow_small_or_imprecise_dtypes=True), inc="s_p")
        E("vector", lambda e: e.memset(spc[:, :], 0.0))
        E("vector", lambda e: e.memset(ones_bf[:, :], 1.0))
        E("vector", lambda e: e.memset(ones_row[0:1, :], 1.0))
        E("vector", lambda e: e.memset(one1[0:1, 0:1], 1.0))
        E("vector", lambda e: e.memset(eps_col[:, :], EPS))
        E("vector", lambda e: e.memset(expb_col[:, :], EXPB))
        P0_V = E("vector", lambda e: e.tensor_scalar(
            out=ident_f[:, :], in0=recip[:, 0:128], scalar1=0.0, scalar2=None,
            op0=ALU.is_equal), waits=[("s_p", 1)], inc="s_v")


        def emit_spacer_inc():
            return E("vector", lambda e: e.tensor_copy(
                spc[:, 128:256], spc[:, 0:128]), inc="s_v")
        def emit_ln(src_ap, first_wait, tp_regs, tp_free0):
            ln_sq = {}
            ln_norm = {}
            ln_tpl = {}
            tev = [tp_free0, tp_free0]
            for tt in range(TT):
                p = tt % 2
                st = stats[p]
                waits = []
                if tt == 0 and first_wait is not None:
                    if isinstance(first_wait, list):
                        waits.extend(first_wait)
                    else:
                        waits.append(first_wait)
                if tt >= 1:
                    waits.append(("s_v", ln_norm[tt - 1]))
                    waits.append(("s_s", ln_sq[tt - 1]))
                nx = E("sync", (lambda tt_: lambda e: e.dma_start(
                    out=x_stage[:, :],
                    in_=src_ap[tt_ * 128:(tt_ + 1) * 128, :]))(tt),
                    waits=waits, inc="s_x", amt=16)
                w2 = [("s_x", nx)]
                if tt >= 1:
                    w2.append(("s_t", ln_tpl[tt - 1]))
                ln_sq[tt] = E("scalar", (lambda p_: lambda e: e.activation(
                    out=xn_tm[:, :], in_=x_stage[:, :], func=AF.Square,
                    accum_out=stats[p_][:, 1:2]))(p), waits=w2, inc="s_s")
                E("vector", (lambda p_: lambda e: e.reduce_sum(
                    stats[p_][:, 0:1], x_stage[:, :], axis=AX.X))(p),
                    waits=[("s_x", nx), ("s_s", ln_sq[tt])])
                E("vector", lambda e: e.drain())
                E("vector", (lambda p_: lambda e: e.tensor_scalar(
                    out=stats[p_][:, 2:3], in0=stats[p_][:, 0:1],
                    scalar1=1.0 / D, scalar2=None, op0=ALU.mult))(p))
                E("vector", lambda e: e.drain())
                E("vector", (lambda p_: lambda e: e.tensor_scalar(
                    out=stats[p_][:, 3:4], in0=stats[p_][:, 1:2],
                    scalar1=1.0 / D, scalar2=None, op0=ALU.mult))(p))
                E("vector", lambda e: e.drain())
                E("vector", (lambda p_: lambda e: e.tensor_tensor(
                    out=stats[p_][:, 7:8], in0=stats[p_][:, 2:3],
                    in1=stats[p_][:, 2:3], op=ALU.mult))(p))
                E("vector", lambda e: e.drain())
                vv = E("vector", (lambda p_: lambda e: e.tensor_tensor(
                    out=stats[p_][:, 3:4], in0=stats[p_][:, 3:4],
                    in1=stats[p_][:, 7:8], op=ALU.subtract))(p), inc="s_v")
                s1 = E("scalar", (lambda p_: lambda e: e.activation(
                    out=stats[p_][:, 4:5], in_=stats[p_][:, 3:4], func=AF.Ln,
                    bias=eps_col[:, :]))(p), waits=[("s_v", vv)], inc="s_s")
                va = E("vector", (lambda p_: lambda e: e.tensor_scalar(
                    out=stats[p_][:, 5:6], in0=stats[p_][:, 4:5],
                    scalar1=-0.5, scalar2=None, op0=ALU.mult))(p),
                    waits=[("s_s", s1)], inc="s_v")
                s2 = E("scalar", (lambda p_: lambda e: e.activation(
                    out=stats[p_][:, 6:7], in_=stats[p_][:, 5:6],
                    func=AF.Exp))(p), waits=[("s_v", va)], inc="s_s")
                ln_norm[tt] = E("vector", (lambda p_: lambda e: e.tensor_scalar(
                    out=xn_tm[:, :], in0=x_stage[:, :],
                    scalar1=stats[p_][:, 2:3], scalar2=stats[p_][:, 6:7],
                    op0=ALU.subtract, op1=ALU.mult))(p),
                    waits=[("s_s", s2), ("s_s", ln_sq[tt])], inc="s_v")
                last_ev = None
                for dc in range(DC):
                    q = dc % 2
                    reg = tp_regs[q]
                    w3 = [("s_v", ln_norm[tt]), ("s_v", tev[q])]
                    tp = E("tensor", (lambda dc_, reg_: lambda e: e.transpose(
                        reg_, xn_tm[:, dc_ * 128:(dc_ + 1) * 128],
                        ident_f[:, :]))(dc, reg), waits=w3, inc="s_t")
                    tev[q] = E("vector", (lambda dc_, tt_, reg_: lambda e: e.tensor_copy(
                        xnf_ap(dc_, tt_ * 128, (tt_ + 1) * 128), reg_))(dc, tt, reg),
                        waits=[("s_t", tp)], inc="s_v")
                    last_ev = tev[q]
                ln_tpl[tt] = V("s_t")
            return last_ev

        ln1_done = emit_ln(x_d, None, (psC[:, 0:128], psD[:, 0:128]), P0_V)

        qkv_mm = {}
        qkv_cast = {}
        qkv_dma = {}
        hg_kst = {}
        hg_v2p0 = {}
        hg_v2p1 = {}
        qkv_free = {}
        v_tp_ev = [ln1_done, ln1_done]
        v2p_dma = [None, None]
        kst_dma = [None]

        def emit_qkv_dma(j):
            w = [("s_v", qkv_cast[j - 2])] if j >= 2 else [("s_v", ln1_done)]
            qkv_dma[j] = E("gpsimd", (lambda j_, pj_: lambda e: e.dma_start(
                out=wstg[pj_][:, :].rearrange("d (c o) -> d c o", o=128),
                in_=wqkv_d[j_].rearrange("c d o -> d c o")))(j, j % 2),
                waits=w, inc=f"s_wg{j % 2}", amt=16)

        def emit_qkv_cast(j):
            wcst = [(f"s_wg{j % 2}", qkv_dma[j])]
            if j >= 4:
                wcst.append(("s_t", qkv_mm[j - 4]))
            E("vector", (lambda pj_, wb_: lambda e: e.tensor_copy(
                wsb[wb_][:, :], wstg[pj_][:, :]))(j % 2, j % 4), waits=wcst)
            qkv_cast[j] = emit_spacer_inc()

        emit_qkv_dma(0)
        emit_qkv_dma(1)
        emit_qkv_cast(0)
        emit_qkv_cast(1)
        for j in range(48):
            pj = j % 2
            wb = [("s_t", qkv_mm[j - 2])] if j >= 2 else []
            nb = E("sync", (lambda j_, pj_: lambda e: e.dma_start(
                out=brow[pj_][0:1, :],
                in_=bqkv_d[j_ * 128:(j_ + 1) * 128][None, :]))(j, pj),
                waits=wb, inc=f"s_br{j % 2}", amt=16)
            wm = [("s_v", qkv_cast[j]), (f"s_br{j % 2}", nb)]
            if j >= 2:
                wm.append(("s_v", qkv_free[j - 2]))
            elif j == 0:
                wm.append(("s_v", ln1_done))
            first = True
            for dc in range(DC):
                for hf in range(2):
                    E("tensor", (lambda pj_, dc_, hf_, wb_: lambda e: e.matmul(
                        psAB[pj_][:, hf_ * 512:(hf_ + 1) * 512],
                        lhsT=wsb[wb_][:, dc_ * 128:(dc_ + 1) * 128],
                        rhs=xnf_ap(dc_, hf_ * 512, (hf_ + 1) * 512),
                        start=(dc_ == 0), stop=False))(pj, dc, hf, j % 4),
                        waits=(wm if first else []))
                    first = False
            for hf in range(2):
                qkv_mm[j] = E("tensor", (lambda pj_, hf_: lambda e: e.matmul(
                    psAB[pj_][:, hf_ * 512:(hf_ + 1) * 512],
                    lhsT=brow[pj_][0:1, :],
                    rhs=ones_row[0:1, 0:512],
                    start=False, stop=True))(pj, hf), inc="s_t")
            if j + 2 < 48:
                emit_qkv_dma(j + 2)
            if j < 16:
                wev = [("s_t", qkv_mm[j])]
                if kst_dma[0] is not None:
                    wev.append(("s_kst", kst_dma[0]))
                ev = E("vector", (lambda pj_: lambda e: e.tensor_copy(
                    kst[pj_][:, :], psAB[pj_][:, :]))(pj),
                    waits=wev, inc="s_v")
                qkv_free[j] = ev
                kr = (j // 4) * 1024 + (j % 4) * 128
                kst_dma[0] = E("sync", (lambda kr_, pj_: lambda e: e.dma_start(
                    out=cc_in.ap()[kr_:kr_ + 128, :],
                    in_=kst[pj_][:, :]))(kr, pj),
                    waits=[("s_v", ev)], inc="s_kst", amt=16)
                hg_kst[j // 4] = kst_dma[0]
            elif j < 32:
                vc = j - 16
                wv = [("s_t", qkv_mm[j])]
                if vc >= 1:
                    wv.append(("s_t", vtp_last[vc - 1]))
                ev0 = E("vector", (lambda pj_: lambda e: e.tensor_copy(
                    xn_tm[:, 0:1024], psAB[pj_][:, :]))(pj), waits=wv, inc="s_v")
                qkv_free[j] = ev0
                for tt in range(TT):
                    q = tt % 2
                    reg = psCD[q][:, 128:256]
                    tp = E("tensor", (lambda tt_, reg_: lambda e: e.transpose(
                        reg_, xn_tm[:, tt_ * 128:(tt_ + 1) * 128],
                        ident_f[:, :]))(tt, reg),
                        waits=[("s_v", ev0), ("s_v", v_tp_ev[q])], inc="s_t")
                    vtp_last[vc] = tp
                    wpe = [("s_t", tp)]
                    if v2p_dma[q] is not None:
                        wpe.append((f"s_v2p{q}", v2p_dma[q]))
                    pe = E("vector", (lambda q_, reg_: lambda e: e.tensor_copy(
                        v2p[q_][:, :], reg_))(q, reg),
                        waits=wpe, inc="s_v")
                    v_tp_ev[q] = pe
                    r0 = (vc // 4) * 1024 + 512 + (vc % 4) * 128
                    v2p_dma[q] = E("sync", (lambda q_, r0_, tt_: lambda e: e.dma_start(
                        out=cc_in.ap()[r0_:r0_ + 128,
                                       tt_ * 128:(tt_ + 1) * 128],
                        in_=v2p[q_][:, :]))(q, r0, tt),
                        waits=[("s_v", pe)], inc=f"s_v2p{q}", amt=16)
                    hg_v2p0[vc // 4] = V("s_v2p0")
                    hg_v2p1[vc // 4] = V("s_v2p1")
            else:
                h = j - 32
                qkv_free[j] = E("vector", (lambda pj_, h_: lambda e: e.tensor_copy(
                    Q_ap(h_, 0, 1024), psAB[pj_][:, :]))(pj, h),
                    waits=[("s_t", qkv_mm[j])], inc="s_v")
            if j + 2 < 48:
                emit_qkv_cast(j + 2)
        for hg in range(4):
            E("gpsimd", (lambda hg_: lambda e: e.collective_compute(
                "AllGather", ALU.bypass,
                replica_groups=[[0, 1, 2, 3], [4, 5, 6, 7]],
                ins=[cc_in.ap()[hg_ * 1024:(hg_ + 1) * 1024, :].opt()],
                outs=[cc_out.ap()[hg_ * 4096:(hg_ + 1) * 4096, :].opt()]))(hg),
                waits=[("s_kst", hg_kst[hg]), ("s_v2p0", hg_v2p0[hg]),
                       ("s_v2p1", hg_v2p1[hg])],
                inc="s_cc", amt=1)
        QKV_LAST_EV = qkv_free[47]

        head_ol = {}
        head_ev = {}
        exp_c = {}
        sg_c = {}
        ol_c = {}
        kv_c = {}
        for h in range(NH):
            hp = h % 2
            wk = [("s_cc", h // 4 + 1)]
            if h >= 2:
                wk.append(("s_t", head_ol[h - 2]))
            for r in range(4):
                base = (h // 4) * 4096 + r * 1024 + (h % 4) * 128
                kv_c[h] = E("sync", (lambda hp_, r_, base_: lambda e: e.dma_start(
                    out=K_ap(hp_, r_ * 1024, (r_ + 1) * 1024),
                    in_=cc_out.ap()[base_:base_ + 128, :]))(hp, r, base),
                    waits=(wk if r == 0 else []), inc=f"s_kv{h % 2}", amt=16)
            for r in range(4):
                base = (h // 4) * 4096 + r * 1024 + 512 + (h % 4) * 128
                kv_c[h] = E("sync", (lambda hp_, r_, base_: lambda e: e.dma_start(
                    out=V_ap(hp_, r_ * 1024, (r_ + 1) * 1024).rearrange(
                        "p (m x) -> p m x", x=128),
                    in_=cc_out.ap()[base_:base_ + 128, :].rearrange(
                        "p (m x) -> p m x", x=128)))(hp, r, base),
                    inc=f"s_kv{h % 2}", amt=16)
            def emit_S(kc):
                gj = h * KC + kc
                pk = gj % 2
                wS = []
                if kc == 0:
                    wS.append((f"s_kv{h % 2}", kv_c[h]))
                if gj >= 2:
                    wS.append(("s_s", exp_c[gj - 2]))
                else:
                    wS.append(("s_v", QKV_LAST_EV))
                for hf in range(2):
                    sg_c[gj] = E("tensor", (lambda h_, hp_, kc_, pk_, hf_: lambda e: e.matmul(
                        psAB[pk_][:, hf_ * 512:(hf_ + 1) * 512],
                        lhsT=K_ap(hp_, kc_ * 128, (kc_ + 1) * 128),
                        rhs=Q_ap(h_, hf_ * 512, (hf_ + 1) * 512),
                        start=True, stop=True))(h, hp, kc, pk, hf),
                        waits=(wS if hf == 0 else []), inc="s_t")

            emit_S(0)
            for kc in range(KC):
                gj = h * KC + kc
                pk = gj % 2
                wE = [("s_t", sg_c[gj])]
                if gj >= 2:
                    wE.append(("s_t", ol_c[gj - 2]))
                exp_c[gj] = E("scalar", (lambda pk_: lambda e: e.activation(
                    out=abuf[pk_][:, :], in_=psAB[pk_][:, :], func=AF.Exp,
                    scale=SCALE, bias=expb_col[:, :]))(pk), waits=wE, inc="s_s")
                if kc + 1 < KC:
                    emit_S(kc + 1)
                wO = [("s_s", exp_c[gj])]
                if kc == 0 and h >= 1:
                    wO.append(("s_v", head_ev[h - 1]))
                for hf in range(2):
                    E("tensor", (lambda hp_, kc_, pk_, hf_: lambda e: e.matmul(
                        psC[:, hf_ * 512:(hf_ + 1) * 512],
                        lhsT=V_ap(hp_, kc_ * 128, (kc_ + 1) * 128),
                        rhs=abuf[pk_][:, hf_ * 512:(hf_ + 1) * 512],
                        start=(kc_ == 0), stop=(kc_ == KC - 1)))(hp, kc, pk, hf),
                        waits=(wO if hf == 0 else []))
                for hf in range(2):
                    ol_c[gj] = E("tensor", (lambda pk_, kc_, hf_: lambda e: e.matmul(
                        psD[:, hf_ * 512:(hf_ + 1) * 512],
                        lhsT=ones_bf[:, :],
                        rhs=abuf[pk_][:, hf_ * 512:(hf_ + 1) * 512],
                        start=(kc_ == 0), stop=(kc_ == KC - 1)))(pk, kc, hf),
                        inc="s_t")
            head_ol[h] = ol_c[h * KC + KC - 1]
            E("vector", lambda e: e.reciprocal(recip[:, :], psD[:, :]),
                waits=[("s_t", head_ol[h])])
            E("vector", lambda e: e.drain())
            head_ev[h] = E("vector", (lambda h_: lambda e: e.tensor_tensor(
                out=O_ap(h_, 0, 1024), in0=psC[:, :], in1=recip[:, :],
                op=ALU.mult))(h), inc="s_v")

        ATT_DONE_T = head_ol[NH - 1]
        ATT_DONE_V = head_ev[NH - 1]

        scl_tp = [None]

        def emit_cols(src_ap1d, n, dst, first_waits):
            nb = E("sync", (lambda n_: lambda e: e.dma_start(
                out=sgrows[0:n_, :],
                in_=src_ap1d.rearrange("(r c) -> r c", c=128)))(n),
                waits=list(first_waits) + (
                    [("s_t", scl_tp[0])] if scl_tp[0] is not None else []),
                inc="s_sg", amt=16)
            E("vector", (lambda n_: lambda e: e.tensor_copy(
                sgrows2[0:n_, :], sgrows[0:n_, :]))(n), waits=[("s_sg", nb)])
            cp = emit_spacer_inc()
            tp = E("tensor", (lambda n_: lambda e: e.transpose(
                psD[:, 512:512 + n_], sgrows2[0:n_, :], ident_f[0:n_, 0:n_]))(n),
                waits=[("s_v", cp)], inc="s_t")
            scl_tp[0] = tp
            return E("vector", (lambda n_, dst_: lambda e: e.tensor_copy(
                dst_[:, 0:n_], psD[:, 512:512 + n_]))(n, dst),
                waits=[("s_t", tp)], inc="s_v")

        bo_ev = emit_cols(bout_d, 16, boc,
                          [("s_t", ATT_DONE_T), ("s_v", ATT_DONE_V)])

        op_mm = {}
        op_cast = {}
        op_ev = {}
        op_tp = {}
        op_tp_ev = [bo_ev, bo_ev]
        last_sp = [None, None, None, None]
        op_dma = {}

        def emit_op_dma(oc):
            w = [("s_v", op_cast[oc - 2])] if oc >= 2 else [("s_t", ATT_DONE_T)]
            op_dma[oc] = E("gpsimd", (lambda oc_, po_: lambda e: e.dma_start(
                out=wstg[po_][:, :].rearrange("d (c o) -> d c o", o=128),
                in_=wout_d[oc_].rearrange("c d o -> d c o")))(oc, oc % 2),
                waits=w, inc=f"s_wg{oc % 2}", amt=16)

        def emit_op_cast(oc):
            wcst = [(f"s_wg{oc % 2}", op_dma[oc])]
            if oc >= 4:
                wcst.append(("s_t", op_mm[oc - 4]))
            else:
                wcst.append(("s_t", qkv_mm[47]))
            E("vector", (lambda po_, wb_: lambda e: e.tensor_copy(
                wsb[wb_][:, :], wstg[po_][:, :]))(oc % 2, oc % 4), waits=wcst)
            op_cast[oc] = emit_spacer_inc()

        emit_op_dma(0)
        emit_op_dma(1)
        emit_op_cast(0)
        emit_op_cast(1)
        op_last_pev = {}
        for oc in range(16):
            po = oc % 2
            wm = [("s_v", op_cast[oc]), ("s_v", bo_ev)]
            if oc >= 1:
                wm.append(("s_v", op_ev[oc - 1]))
            first = True
            for hc in range(16):
                for hf in range(2):
                    op_mm[oc] = E("tensor", (lambda po_, hc_, hf_, wb_: lambda e: e.matmul(
                        psAB[po_][:, hf_ * 512:(hf_ + 1) * 512],
                        lhsT=wsb[wb_][:, hc_ * 128:(hc_ + 1) * 128],
                        rhs=O_ap(hc_, hf_ * 512, (hf_ + 1) * 512),
                        start=(hc_ == 0), stop=(hc_ == 15)))(po, hc, hf, oc % 4),
                        waits=(wm if first else []), inc="s_t")
                    first = False
            if oc + 2 < 16:
                emit_op_dma(oc + 2)
                emit_op_cast(oc + 2)
            wxc = [("s_v", op_last_pev[oc - 1])] if oc >= 1 else \
                  [("s_t", ATT_DONE_T), ("s_v", ATT_DONE_V)]
            nxc = E("sync", (lambda oc_: lambda e: e.dma_start(
                out=recip[:, :].rearrange("p (tt o) -> p tt o", o=128),
                in_=x_d[:, oc_ * 128:(oc_ + 1) * 128].rearrange(
                    "(tt p) o -> p tt o", p=128)))(oc),
                waits=wxc, inc="s_xs0", amt=16)
            half = oc % 2
            wv = [("s_t", op_mm[oc])]
            if oc >= 2:
                wv.append(("s_t", op_tp[oc - 2]))
            op_ev[oc] = E("vector", (lambda oc_, po_, half_: lambda e: e.tensor_scalar(
                out=xn_tm[:, half_ * 1024:(half_ + 1) * 1024],
                in0=psAB[po_][:, :], scalar1=boc[:, oc_:oc_ + 1], scalar2=None,
                op0=ALU.add))(oc, po, half), waits=wv, inc="s_v")
            for tt in range(TT):
                q = tt % 2
                q4 = tt % 4
                reg = psCD[q][:, 256:384]
                wtp = [("s_v", op_ev[oc]), ("s_v", op_tp_ev[q])]
                op_tp[oc] = E("tensor", (lambda half_, tt_, reg_: lambda e: e.transpose(
                    reg_,
                    xn_tm[:, half_ * 1024 + tt_ * 128:half_ * 1024 + (tt_ + 1) * 128],
                    ident_f[:, :]))(half, tt, reg), waits=wtp, inc="s_t")
                wev = [("s_t", op_tp[oc]), ("s_xs0", nxc)]
                if last_sp[q4] is not None:
                    wev.append((f"s_sp{q4}", last_sp[q4]))
                ev = E("vector", (lambda q4_, tt_, reg_: lambda e: e.tensor_tensor(
                    out=x2p[q4_][:, :], in0=reg_,
                    in1=recip[:, tt_ * 128:(tt_ + 1) * 128],
                    op=ALU.add))(q4, tt, reg), waits=wev, inc="s_v")
                op_tp_ev[q] = ev
                op_last_pev[oc] = ev
                last_sp[q4] = E("sync", (lambda oc_, tt_, q4_: lambda e: e.dma_start(
                    out=x2sp.ap()[tt_ * 128:(tt_ + 1) * 128,
                                  oc_ * 128:(oc_ + 1) * 128],
                    in_=x2p[q4_][:, :]))(oc, tt, q4),
                    waits=[("s_v", ev)], inc=f"s_sp{q4}", amt=16)
        SPILL_W = [(f"s_sp{i}", V(f"s_sp{i}")) for i in range(4)]

        ln2_done = emit_ln(x2sp.ap(), SPILL_W,
                           (psC[:, 0:128], psD[:, 0:128]),
                           max(op_tp_ev))

        ev_sg = emit_cols(sg_d, HC, sgc, [("s_v", ln2_done)])
        ev_su = emit_cols(su_d, HC, suc, [("s_v", ev_sg)])
        ev_sd = emit_cols(sd_d, 16, sdc, [("s_v", ev_su)])

        g_mm = {}
        u_mm = {}
        silu_c = {}
        uev_c = {}
        pmul_c = {}
        cast_g = {}
        cast_u = {}
        for hc in range(HC):
            ph = hc % 2
            wwg = [("s_v", ev_sd)] if hc == 0 else [("s_v", cast_u[hc - 1])]
            ng = E("gpsimd", (lambda hc_: lambda e: e.dma_start(
                out=stg[0][:, :].rearrange("d (c o) -> d c o", o=128),
                in_=wg_d[hc_].rearrange("c d o -> d c o")))(hc),
                waits=wwg, inc="s_wg0", amt=16)
            nu = E("gpsimd", (lambda hc_: lambda e: e.dma_start(
                out=stg[1][:, :].rearrange("d (c o) -> d c o", o=128),
                in_=wu_d[hc_].rearrange("c d o -> d c o")))(hc),
                inc="s_wg1", amt=16)
            wcg = [("s_wg0", ng)]
            if hc >= 2:
                wcg.append(("s_t", g_mm[hc - 2]))
            else:
                wcg.append(("s_t", op_mm[15]))
            E("vector", (lambda hc_: lambda e: e.tensor_scalar(
                out=wsb[hc_ % 2][:, :], in0=stg[0][:, :], scalar1=128.0,
                scalar2=None, op0=ALU.subtract))(hc), waits=wcg)
            cast_g[hc] = emit_spacer_inc()
            wcu = [("s_wg1", nu)]
            if hc >= 2:
                wcu.append(("s_t", u_mm[hc - 2]))
            E("vector", (lambda hc_: lambda e: e.tensor_scalar(
                out=wsb[2 + hc_ % 2][:, :], in0=stg[1][:, :], scalar1=128.0,
                scalar2=None, op0=ALU.subtract))(hc), waits=wcu)
            cast_u[hc] = emit_spacer_inc()
            wgm = [("s_v", cast_g[hc])]
            if hc >= 2:
                wgm.append(("s_s", silu_c[hc - 2]))
            if hc < 2:
                wgm.append(("s_v", ln2_done))
            first = True
            for dc in range(DC):
                for hf in range(2):
                    g_mm[hc] = E("tensor", (lambda ph_, dc_, hf_, wb_=hc % 2: lambda e: e.matmul(
                        psAB[ph_][:, hf_ * 512:(hf_ + 1) * 512],
                        lhsT=wsb[wb_][:, dc_ * 128:(dc_ + 1) * 128],
                        rhs=xnf_ap(dc_, hf_ * 512, (hf_ + 1) * 512),
                        start=(dc_ == 0), stop=(dc_ == DC - 1)))(ph, dc, hf),
                        waits=(wgm if first else []), inc="s_t")
                    first = False
            wsi = [("s_t", g_mm[hc]), ("s_v", ev_sg)]
            if hc >= 1:
                wsi.append(("s_v", pmul_c[hc - 1]))
            silu_c[hc] = E("scalar", (lambda ph_, hc_: lambda e: e.activation(
                out=sgt[:, :], in_=psAB[ph_][:, :], func=AF.Silu,
                scale=sgc[:, hc_:hc_ + 1]))(ph, hc), waits=wsi, inc="s_s")
            wum = [("s_v", cast_u[hc])]
            if hc >= 2:
                wum.append(("s_v", uev_c[hc - 2]))
            if hc == 0:
                wum.append(("s_v", ln2_done))
            elif hc == 1:
                wum.append(("s_v", ev_sd))
            first = True
            for dc in range(DC):
                for hf in range(2):
                    u_mm[hc] = E("tensor", (lambda ph_, dc_, hf_, wb_=2 + hc % 2: lambda e: e.matmul(
                        psCD[ph_][:, hf_ * 512:(hf_ + 1) * 512],
                        lhsT=wsb[wb_][:, dc_ * 128:(dc_ + 1) * 128],
                        rhs=xnf_ap(dc_, hf_ * 512, (hf_ + 1) * 512),
                        start=(dc_ == 0), stop=(dc_ == DC - 1)))(ph, dc, hf),
                        waits=(wum if first else []), inc="s_t")
                    first = False
            uev_c[hc] = E("vector", (lambda ph_, hc_: lambda e: e.tensor_scalar(
                out=utmp[:, :], in0=psCD[ph_][:, :], scalar1=suc[:, hc_:hc_ + 1],
                scalar2=None, op0=ALU.mult))(ph, hc),
                waits=[("s_t", u_mm[hc]), ("s_v", ev_su)], inc="s_v")
            pmul_c[hc] = E("vector", (lambda hc_: lambda e: e.tensor_tensor(
                out=P_ap(hc_, 0, 1024), in0=sgt[:, :], in1=utmp[:, :],
                op=ALU.mult))(hc), waits=[("s_s", silu_c[hc])], inc="s_v")
        FFN_GU_V = pmul_c[HC - 1]
        FFN_GU_T = max(g_mm[HC - 1], u_mm[HC - 1])

        dn_piece_mm = {}
        dn_piece_cast = {}
        dn_last_pev = {}
        dn_ev = {}
        dn_tp = {}
        dn_tp_ev = [None, None]
        last_out = [None, None, None, None]
        for oc in range(16):
            po = oc % 2
            for pc in range(4):
                idx = oc * 4 + pc
                sp = idx % 2
                sw = idx % 4
                wwd = []
                if idx >= 2:
                    wwd.append(("s_v", dn_piece_cast[idx - 2]))
                else:
                    wwd.append(("s_v", FFN_GU_V))
                nd = E("gpsimd", (lambda oc_, pc_, sp_: lambda e: e.dma_start(
                    out=stg[sp_][:, 0:1408].rearrange("d (c o) -> d c o", o=128),
                    in_=wd_d[oc_, pc_ * 11:(pc_ + 1) * 11].rearrange(
                        "c d o -> d c o")))(oc, pc, sp),
                    waits=wwd, inc=f"s_wg{sp}", amt=16)
                wcd = [(f"s_wg{sp}", nd)]
                if idx >= 4:
                    wcd.append(("s_t", dn_piece_mm[idx - 4]))
                else:
                    wcd.append(("s_t", FFN_GU_T))
                E("vector", (lambda sp_, sw_: lambda e: e.tensor_scalar(
                    out=wsb[sw_][:, 0:1408], in0=stg[sp_][:, 0:1408],
                    scalar1=128.0, scalar2=None, op0=ALU.subtract))(sp, sw),
                    waits=wcd)
                dn_piece_cast[idx] = emit_spacer_inc()
                first = True
                for ti in range(11):
                    hcg = pc * 11 + ti
                    wmm = []
                    if first:
                        wmm.append(("s_v", dn_piece_cast[idx]))
                        if pc == 0:
                            if oc >= 1:
                                wmm.append(("s_v", dn_ev[oc - 1]))
                            else:
                                wmm.append(("s_v", FFN_GU_V))
                    for hf in range(2):
                        dn_piece_mm[idx] = E("tensor",
                            (lambda po_, sw_, ti_, hcg_, hf_: lambda e: e.matmul(
                                psAB[po_][:, hf_ * 512:(hf_ + 1) * 512],
                                lhsT=wsb[sw_][:, ti_ * 128:(ti_ + 1) * 128],
                                rhs=P_ap(hcg_, hf_ * 512, (hf_ + 1) * 512),
                                start=(hcg_ == 0), stop=(hcg_ == HC - 1)))(
                                    po, sw, ti, hcg, hf),
                            waits=(wmm if hf == 0 and wmm else []), inc="s_t")
                        first = False
            dn_mm_last = dn_piece_mm[oc * 4 + 3]
            wxc = [("s_v", dn_last_pev[oc - 1])] if oc >= 1 else \
                  [("s_v", FFN_GU_V)]
            nxc = E("sync", (lambda oc_: lambda e: e.dma_start(
                out=recip[:, :].rearrange("p (tt o) -> p tt o", o=128),
                in_=x2sp.ap()[:, oc_ * 128:(oc_ + 1) * 128].rearrange(
                    "(tt p) o -> p tt o", p=128)))(oc),
                waits=wxc, inc="s_xs1", amt=16)
            half = oc % 2
            wv = [("s_t", dn_mm_last), ("s_v", ev_sd)]
            if oc >= 2:
                wv.append(("s_t", dn_tp[oc - 2]))
            dn_ev[oc] = E("vector", (lambda oc_, po_, half_: lambda e: e.tensor_scalar(
                out=xn_tm[:, half_ * 1024:(half_ + 1) * 1024],
                in0=psAB[po_][:, :], scalar1=sdc[:, oc_:oc_ + 1], scalar2=None,
                op0=ALU.mult))(oc, po, half), waits=wv, inc="s_v")
            for tt in range(TT):
                q = tt % 2
                q4 = tt % 4
                reg = psCD[q][:, 384:512]
                wtp = [("s_v", dn_ev[oc])]
                if dn_tp_ev[q] is not None:
                    wtp.append(("s_v", dn_tp_ev[q]))
                dn_tp[oc] = E("tensor", (lambda half_, tt_, reg_: lambda e: e.transpose(
                    reg_,
                    xn_tm[:, half_ * 1024 + tt_ * 128:half_ * 1024 + (tt_ + 1) * 128],
                    ident_f[:, :]))(half, tt, reg), waits=wtp, inc="s_t")
                wev = [("s_t", dn_tp[oc]), ("s_xs1", nxc)]
                if last_out[q4] is not None:
                    wev.append((f"s_out{q4}", last_out[q4]))
                ev = E("vector", (lambda q4_, tt_, reg_: lambda e: e.tensor_tensor(
                    out=x2p[q4_][:, :], in0=reg_,
                    in1=recip[:, tt_ * 128:(tt_ + 1) * 128],
                    op=ALU.add))(q4, tt, reg), waits=wev, inc="s_v")
                dn_tp_ev[q] = ev
                dn_last_pev[oc] = ev
                last_out[q4] = E("sync", (lambda oc_, tt_, q4_: lambda e: e.dma_start(
                    out=out_d[tt_ * 128:(tt_ + 1) * 128,
                              oc_ * 128:(oc_ + 1) * 128],
                    in_=x2p[q4_][:, :]))(oc, tt, q4),
                    waits=[("s_v", ev)], inc=f"s_out{q4}", amt=16)

        for i in range(4):
            E("sync", (lambda i_: lambda e: e.wait_ge(
                sems[f"s_out{i_}"], V(f"s_out{i_}")))(i))

        def emit_engine(eng_obj, name):
            for waits, fn, inc, amt in ops[name]:
                for (s, v) in waits:
                    eng_obj.wait_ge(sems[s], v)
                instr = fn(eng_obj)
                if inc is None:
                    continue
                if name in ("vector", "scalar") or (name == "gpsimd" and inc == "s_p"):
                    eng_obj.drain().then_inc(sems[inc], amt)
                elif instr is not None:
                    instr.then_inc(sems[inc], amt)

        @block.sync
        def _(sync):
            emit_engine(sync, "sync")

        @block.gpsimd
        def _(gpsimd):
            emit_engine(gpsimd, "gpsimd")

        @block.tensor
        def _(tensor):
            emit_engine(tensor, "tensor")

        @block.scalar
        def _(scalar):
            emit_engine(scalar, "scalar")

        @block.vector
        def _(vector):
            emit_engine(vector, "vector")

    return nc


vtp_last = {}


def kernel(L0_attn_hidden_states, L0_attn_w_qkv, L0_attn_b_qkv, L0_attn_w_out,
           L0_attn_b_out, L0_ffn_wq_gate, L0_ffn_s_gate, L0_ffn_wq_up,
           L0_ffn_s_up, L0_ffn_wq_down, L0_ffn_s_down):
    global vtp_last
    vtp_last = {}

    x = np.ascontiguousarray(np.asarray(L0_attn_hidden_states, dtype=np.float32))
    w_qkv = np.asarray(L0_attn_w_qkv, dtype=np.float32)
    b_qkv = np.asarray(L0_attn_b_qkv, dtype=np.float32)
    w_out = np.asarray(L0_attn_w_out, dtype=np.float32)
    b_out = np.ascontiguousarray(np.asarray(L0_attn_b_out, dtype=np.float32))
    wq_g = np.asarray(L0_ffn_wq_gate, dtype=np.int32)
    wq_u = np.asarray(L0_ffn_wq_up, dtype=np.int32)
    wq_d = np.asarray(L0_ffn_wq_down, dtype=np.int32)
    s_g = np.ascontiguousarray(np.asarray(L0_ffn_s_gate, dtype=np.float32))
    s_u = np.ascontiguousarray(np.asarray(L0_ffn_s_up, dtype=np.float32))
    s_d = np.ascontiguousarray(np.asarray(L0_ffn_s_down, dtype=np.float32))

    WT = np.ascontiguousarray(w_qkv.T)
    blocks = []
    for lo in (2048, 4096, 0):
        blocks.append(WT[:, lo:lo + 2048].reshape(DC, 128, 16, 128)
                      .transpose(2, 0, 1, 3))
    wqkv_b = np.ascontiguousarray(np.concatenate(blocks, axis=0))
    bqkvr = np.ascontiguousarray(
        np.concatenate([b_qkv[2048:4096], b_qkv[4096:6144], b_qkv[0:2048]]))

    wout_b = np.ascontiguousarray(
        np.ascontiguousarray(w_out.T).reshape(16, 128, 16, 128)
        .transpose(2, 0, 1, 3))
    wg_b = np.ascontiguousarray(
        np.ascontiguousarray(wq_g.T).reshape(DC, 128, HC, 128)
        .transpose(2, 0, 1, 3))
    wu_b = np.ascontiguousarray(
        np.ascontiguousarray(wq_u.T).reshape(DC, 128, HC, 128)
        .transpose(2, 0, 1, 3))
    wd_b = np.ascontiguousarray(
        np.ascontiguousarray(wq_d.T).reshape(HC, 128, 16, 128)
        .transpose(2, 0, 1, 3))

    nc = build_nc()
    in_maps = []
    for c in range(NCORE):
        bb = c // 4
        t0 = (c % 4) * T
        in_maps.append({
            "x": np.ascontiguousarray(x[bb, t0:t0 + T, :]),
            "wqkv": wqkv_b,
            "wo": wout_b,
            "bqkvr": bqkvr,
            "bo": b_out,
            "wg": wg_b,
            "wu": wu_b,
            "wd": wd_b,
            "sg": s_g,
            "su": s_u,
            "sd": s_d,
        })
    res = run_bass_kernel_spmd(nc, in_maps, core_ids=list(range(NCORE)))
    out = np.empty((B, S, D), dtype=np.float32)
    for c in range(NCORE):
        bb = c // 4
        t0 = (c % 4) * T
        out[bb, t0:t0 + T, :] = res.results[c]["out"]
    return out


# version 16
# speedup vs baseline: 1.1064x; 1.1064x over previous
import numpy as np
from contextlib import ExitStack

import concourse.bass as bass
import concourse.mybir as mybir
from concourse.bass_utils import run_bass_kernel_spmd

F32 = mybir.dt.float32
BF16 = mybir.dt.bfloat16
I32 = mybir.dt.int32
AF = mybir.ActivationFunctionType
ALU = mybir.AluOpType
AX = mybir.AxisListType

B, S, D = 2, 4096, 2048
NCORE = 8
T = 1024
TT = 8
DC = 16
NH = 16
HD = 128
HID = 5632
HC = 44
KC = 32
SCALE = 1.0 / float(np.sqrt(HD))
EXPB = -20.0
EPS = 1e-5

ENGINES = ("sync", "gpsimd", "tensor", "scalar", "vector")


def build_nc():
    nc = bass.Bass("TRN2", target_bir_lowering=False, debug=False)

    x_d = nc.dram_tensor("x", [T, D], F32, kind="ExternalInput").ap()
    wqkv_d = nc.dram_tensor("wqkv", [48, DC, 128, 128], F32, kind="ExternalInput").ap()
    wout_d = nc.dram_tensor("wo", [16, 16, 128, 128], F32, kind="ExternalInput").ap()
    bqkv_d = nc.dram_tensor("bqkvr", [48 * 128], F32, kind="ExternalInput").ap()
    bout_d = nc.dram_tensor("bo", [D], F32, kind="ExternalInput").ap()
    wg_d = nc.dram_tensor("wg", [HC, DC, 128, 128], I32, kind="ExternalInput").ap()
    wu_d = nc.dram_tensor("wu", [HC, DC, 128, 128], I32, kind="ExternalInput").ap()
    wd_d = nc.dram_tensor("wd", [16, HC, 128, 128], I32, kind="ExternalInput").ap()
    sg_d = nc.dram_tensor("sg", [HID], F32, kind="ExternalInput").ap()
    su_d = nc.dram_tensor("su", [HID], F32, kind="ExternalInput").ap()
    sd_d = nc.dram_tensor("sd", [D], F32, kind="ExternalInput").ap()
    out_d = nc.dram_tensor("out", [T, D], F32, kind="ExternalOutput").ap()

    cc_in = nc.dram_tensor("cc_in", [4096, T], BF16)
    cc_out = nc.dram_tensor("cc_out", [4 * 4096, T], BF16)
    x2sp = nc.dram_tensor("x2sp", [T, D], F32)

    ops = {e: [] for e in ENGINES}
    cnt = {}

    def E(eng, fn, waits=(), inc=None, amt=1):
        ops[eng].append((list(waits), fn, inc, amt))
        if inc is not None:
            cnt[inc] = cnt.get(inc, 0) + amt
            return cnt[inc]
        return None

    def V(sem):
        return cnt.get(sem, 0)

    with ExitStack() as stack:
        def sb(name, shape, dt):
            return stack.enter_context(nc.sbuf_tensor(name, shape, dt))

        def psum(name, shape, dt):
            return stack.enter_context(nc.psum_tensor(name, shape, dt))

        arenaA = sb("arenaA", [128, 49152], BF16)
        xnf = sb("xnf", [128, 16384], BF16)
        stg = [sb(f"stg{i}", [128, 2048], I32) for i in range(2)]
        wstg = [sb(f"wstg{i}", [128, 2048], F32) for i in range(2)]
        x_stage = wstg[0]
        wsb = [sb(f"wsb{i}", [128, 2048], BF16) for i in range(4)]
        xn_tm = sb("xn_tm", [128, 2048], F32)
        abuf = [sb(f"abuf{i}", [128, 1024], BF16) for i in range(2)]
        recip = sb("recip", [128, 1024], F32)
        kst = [sb("kst0", [128, 1024], BF16)]
        kst = [kst[0], kst[0]]
        v2p = [sb(f"v2p{i}", [128, 128], BF16) for i in range(2)]
        brow = [sb(f"brow{i}", [1, 128], F32) for i in range(2)]
        spc = sb("spc", [128, 256], BF16)
        sgrows = sb("sgrows", [64, 128], F32)
        sgrows2 = sb("sgrows2", [64, 128], F32)
        utmp = sb("utmp", [128, 1024], BF16)
        sgt = sb("sgt", [128, 1024], BF16)
        ones_bf = sb("ones_bf", [128, 128], BF16)
        ident_f = sb("ident_f", [128, 128], F32)

        ones_row = sb("ones_row", [1, 512], F32)
        one1 = sb("one1", [1, 1], F32)
        sgc = sb("sgc", [128, HC], F32)
        suc = sb("suc", [128, HC], F32)
        sdc = sb("sdc", [128, 16], F32)
        boc = sb("boc", [128, 16], F32)
        stats = [sb(f"stats{i}", [128, 8], F32) for i in range(2)]
        eps_col = sb("eps_col", [128, 1], F32)
        expb_col = sb("expb_col", [128, 1], F32)
        x2p = [sb(f"x2p{i}", [128, 128], F32) for i in range(4)]

        psA = psum("psA", [128, 1024], F32)
        psB = psum("psB", [128, 1024], F32)
        psC = psum("psC", [128, 1024], F32)
        psD = psum("psD", [128, 1024], F32)
        psAB = [psA, psB]
        psCD = [psC, psD]

        sems = {}
        for sname in ("s_x", "s_br0", "s_br1", "s_kv0", "s_kv1",
                      "s_wg0", "s_wg1", "s_cc", "s_sg",
                      "s_sp0", "s_sp1", "s_sp2", "s_sp3",
                      "s_out0", "s_out1", "s_out2", "s_out3",
                      "s_xs0", "s_xs1", "s_kst", "s_v2p0", "s_v2p1",
                      "s_t", "s_s", "s_v", "s_p"):
            sems[sname] = stack.enter_context(nc.semaphore(sname))

        block = stack.enter_context(nc.Block())

        QCOL, OCOL, KCOL, VCOL = 0, 16384, 32768, 40960

        def Q_ap(h, c0, c1):
            return arenaA[:, QCOL + h * 1024 + c0:QCOL + h * 1024 + c1]

        def O_ap(h, c0, c1):
            return arenaA[:, OCOL + h * 1024 + c0:OCOL + h * 1024 + c1]

        def K_ap(hp, c0, c1):
            return arenaA[:, KCOL + hp * 4096 + c0:KCOL + hp * 4096 + c1]

        def V_ap(hp, c0, c1):
            return arenaA[:, VCOL + hp * 4096 + c0:VCOL + hp * 4096 + c1]

        def P_ap(hc, c0, c1):
            return arenaA[:, hc * 1024 + c0:hc * 1024 + c1]

        def xnf_ap(dc, c0, c1):
            return xnf[:, dc * 1024 + c0:dc * 1024 + c1]

        E("gpsimd", lambda e: e.iota(recip[:, 0:128], pattern=[[1, 128]], base=0,
                                     channel_multiplier=-1,
                                     allow_small_or_imprecise_dtypes=True), inc="s_p")
        E("vector", lambda e: e.memset(spc[:, :], 0.0))
        E("vector", lambda e: e.memset(ones_bf[:, :], 1.0))
        E("vector", lambda e: e.memset(ones_row[0:1, :], 1.0))
        E("vector", lambda e: e.memset(one1[0:1, 0:1], 1.0))
        E("vector", lambda e: e.memset(eps_col[:, :], EPS))
        E("vector", lambda e: e.memset(expb_col[:, :], EXPB))
        P0_V = E("vector", lambda e: e.tensor_scalar(
            out=ident_f[:, :], in0=recip[:, 0:128], scalar1=0.0, scalar2=None,
            op0=ALU.is_equal), waits=[("s_p", 1)], inc="s_v")


        def emit_spacer_inc():
            return E("vector", lambda e: e.tensor_copy(
                spc[:, 128:256], spc[:, 0:128]), inc="s_v")
        def emit_ln(src_ap, first_wait, tp_regs, tp_free0):
            ln_sq = {}
            ln_norm = {}
            ln_tpl = {}
            tev = [tp_free0, tp_free0]
            for tt in range(TT):
                p = tt % 2
                st = stats[p]
                waits = []
                if tt == 0 and first_wait is not None:
                    if isinstance(first_wait, list):
                        waits.extend(first_wait)
                    else:
                        waits.append(first_wait)
                if tt >= 1:
                    waits.append(("s_v", ln_norm[tt - 1]))
                    waits.append(("s_s", ln_sq[tt - 1]))
                nx = E("sync", (lambda tt_: lambda e: e.dma_start(
                    out=x_stage[:, :],
                    in_=src_ap[tt_ * 128:(tt_ + 1) * 128, :]))(tt),
                    waits=waits, inc="s_x", amt=16)
                w2 = [("s_x", nx)]
                if tt >= 1:
                    w2.append(("s_t", ln_tpl[tt - 1]))
                ln_sq[tt] = E("scalar", (lambda p_: lambda e: e.activation(
                    out=xn_tm[:, :], in_=x_stage[:, :], func=AF.Square,
                    accum_out=stats[p_][:, 1:2]))(p), waits=w2, inc="s_s")
                E("vector", (lambda p_: lambda e: e.reduce_sum(
                    stats[p_][:, 0:1], x_stage[:, :], axis=AX.X))(p),
                    waits=[("s_x", nx), ("s_s", ln_sq[tt])])
                E("vector", lambda e: e.drain())
                E("vector", (lambda p_: lambda e: e.tensor_scalar(
                    out=stats[p_][:, 2:3], in0=stats[p_][:, 0:1],
                    scalar1=1.0 / D, scalar2=None, op0=ALU.mult))(p))
                E("vector", lambda e: e.drain())
                E("vector", (lambda p_: lambda e: e.tensor_scalar(
                    out=stats[p_][:, 3:4], in0=stats[p_][:, 1:2],
                    scalar1=1.0 / D, scalar2=None, op0=ALU.mult))(p))
                E("vector", lambda e: e.drain())
                E("vector", (lambda p_: lambda e: e.tensor_tensor(
                    out=stats[p_][:, 7:8], in0=stats[p_][:, 2:3],
                    in1=stats[p_][:, 2:3], op=ALU.mult))(p))
                E("vector", lambda e: e.drain())
                vv = E("vector", (lambda p_: lambda e: e.tensor_tensor(
                    out=stats[p_][:, 3:4], in0=stats[p_][:, 3:4],
                    in1=stats[p_][:, 7:8], op=ALU.subtract))(p), inc="s_v")
                s1 = E("scalar", (lambda p_: lambda e: e.activation(
                    out=stats[p_][:, 4:5], in_=stats[p_][:, 3:4], func=AF.Ln,
                    bias=eps_col[:, :]))(p), waits=[("s_v", vv)], inc="s_s")
                va = E("vector", (lambda p_: lambda e: e.tensor_scalar(
                    out=stats[p_][:, 5:6], in0=stats[p_][:, 4:5],
                    scalar1=-0.5, scalar2=None, op0=ALU.mult))(p),
                    waits=[("s_s", s1)], inc="s_v")
                s2 = E("scalar", (lambda p_: lambda e: e.activation(
                    out=stats[p_][:, 6:7], in_=stats[p_][:, 5:6],
                    func=AF.Exp))(p), waits=[("s_v", va)], inc="s_s")
                ln_norm[tt] = E("vector", (lambda p_: lambda e: e.tensor_scalar(
                    out=xn_tm[:, :], in0=x_stage[:, :],
                    scalar1=stats[p_][:, 2:3], scalar2=stats[p_][:, 6:7],
                    op0=ALU.subtract, op1=ALU.mult))(p),
                    waits=[("s_s", s2), ("s_s", ln_sq[tt])], inc="s_v")
                last_ev = None
                for dc in range(DC):
                    q = dc % 2
                    reg = tp_regs[q]
                    w3 = [("s_v", ln_norm[tt]), ("s_v", tev[q])]
                    tp = E("tensor", (lambda dc_, reg_: lambda e: e.transpose(
                        reg_, xn_tm[:, dc_ * 128:(dc_ + 1) * 128],
                        ident_f[:, :]))(dc, reg), waits=w3, inc="s_t")
                    tev[q] = E("vector", (lambda dc_, tt_, reg_: lambda e: e.tensor_copy(
                        xnf_ap(dc_, tt_ * 128, (tt_ + 1) * 128), reg_))(dc, tt, reg),
                        waits=[("s_t", tp)], inc="s_v")
                    last_ev = tev[q]
                ln_tpl[tt] = V("s_t")
            return last_ev

        ln1_done = emit_ln(x_d, None, (psC[:, 0:128], psD[:, 0:128]), P0_V)

        qkv_mm = {}
        qkv_cast = {}
        qkv_dma = {}
        hg_kst = {}
        hg_v2p0 = {}
        hg_v2p1 = {}
        qkv_free = {}
        v_tp_ev = [ln1_done, ln1_done]
        v2p_dma = [None, None]
        kst_dma = [None]

        def emit_qkv_dma(j):
            if j >= 2:
                w = [("s_v", qkv_cast[j - 2])]
            elif j == 0:
                w = [("s_v", ln1_done)]
            else:
                w = []
            qkv_dma[j] = E("gpsimd", (lambda j_, pj_: lambda e: e.dma_start(
                out=wstg[pj_][:, :].rearrange("d (c o) -> d c o", o=128),
                in_=wqkv_d[j_].rearrange("c d o -> d c o")))(j, j % 2),
                waits=w, inc=f"s_wg{j % 2}", amt=16)

        def emit_qkv_cast(j):
            wcst = [(f"s_wg{j % 2}", qkv_dma[j])]
            if j >= 4:
                wcst.append(("s_t", qkv_mm[j - 4]))
            E("vector", (lambda pj_, wb_: lambda e: e.tensor_copy(
                wsb[wb_][:, :], wstg[pj_][:, :]))(j % 2, j % 4), waits=wcst)
            qkv_cast[j] = emit_spacer_inc()

        emit_qkv_dma(1)
        emit_qkv_dma(0)
        emit_qkv_cast(1)
        emit_qkv_cast(0)
        for j in range(48):
            pj = j % 2
            wb = [("s_t", qkv_mm[j - 2])] if j >= 2 else []
            nb = E("sync", (lambda j_, pj_: lambda e: e.dma_start(
                out=brow[pj_][0:1, :],
                in_=bqkv_d[j_ * 128:(j_ + 1) * 128][None, :]))(j, pj),
                waits=wb, inc=f"s_br{j % 2}", amt=16)
            wm = [("s_v", qkv_cast[j]), (f"s_br{j % 2}", nb)]
            if j >= 2:
                wm.append(("s_v", qkv_free[j - 2]))
            elif j == 0:
                wm.append(("s_v", ln1_done))
            first = True
            for dc in range(DC):
                for hf in range(2):
                    E("tensor", (lambda pj_, dc_, hf_, wb_: lambda e: e.matmul(
                        psAB[pj_][:, hf_ * 512:(hf_ + 1) * 512],
                        lhsT=wsb[wb_][:, dc_ * 128:(dc_ + 1) * 128],
                        rhs=xnf_ap(dc_, hf_ * 512, (hf_ + 1) * 512),
                        start=(dc_ == 0), stop=False))(pj, dc, hf, j % 4),
                        waits=(wm if first else []))
                    first = False
            for hf in range(2):
                qkv_mm[j] = E("tensor", (lambda pj_, hf_: lambda e: e.matmul(
                    psAB[pj_][:, hf_ * 512:(hf_ + 1) * 512],
                    lhsT=brow[pj_][0:1, :],
                    rhs=ones_row[0:1, 0:512],
                    start=False, stop=True))(pj, hf), inc="s_t")
            if j + 2 < 48:
                emit_qkv_dma(j + 2)
            if j < 16:
                wev = [("s_t", qkv_mm[j])]
                if kst_dma[0] is not None:
                    wev.append(("s_kst", kst_dma[0]))
                ev = E("vector", (lambda pj_: lambda e: e.tensor_copy(
                    kst[pj_][:, :], psAB[pj_][:, :]))(pj),
                    waits=wev, inc="s_v")
                qkv_free[j] = ev
                kr = (j // 4) * 1024 + (j % 4) * 128
                kst_dma[0] = E("sync", (lambda kr_, pj_: lambda e: e.dma_start(
                    out=cc_in.ap()[kr_:kr_ + 128, :],
                    in_=kst[pj_][:, :]))(kr, pj),
                    waits=[("s_v", ev)], inc="s_kst", amt=16)
                hg_kst[j // 4] = kst_dma[0]
            elif j < 32:
                vc = j - 16
                wv = [("s_t", qkv_mm[j])]
                if vc >= 1:
                    wv.append(("s_t", vtp_last[vc - 1]))
                ev0 = E("vector", (lambda pj_: lambda e: e.tensor_copy(
                    xn_tm[:, 0:1024], psAB[pj_][:, :]))(pj), waits=wv, inc="s_v")
                qkv_free[j] = ev0
                for tt in range(TT):
                    q = tt % 2
                    reg = psCD[q][:, 128:256]
                    tp = E("tensor", (lambda tt_, reg_: lambda e: e.transpose(
                        reg_, xn_tm[:, tt_ * 128:(tt_ + 1) * 128],
                        ident_f[:, :]))(tt, reg),
                        waits=[("s_v", ev0), ("s_v", v_tp_ev[q])], inc="s_t")
                    vtp_last[vc] = tp
                    wpe = [("s_t", tp)]
                    if v2p_dma[q] is not None:
                        wpe.append((f"s_v2p{q}", v2p_dma[q]))
                    pe = E("vector", (lambda q_, reg_: lambda e: e.tensor_copy(
                        v2p[q_][:, :], reg_))(q, reg),
                        waits=wpe, inc="s_v")
                    v_tp_ev[q] = pe
                    r0 = (vc // 4) * 1024 + 512 + (vc % 4) * 128
                    v2p_dma[q] = E("sync", (lambda q_, r0_, tt_: lambda e: e.dma_start(
                        out=cc_in.ap()[r0_:r0_ + 128,
                                       tt_ * 128:(tt_ + 1) * 128],
                        in_=v2p[q_][:, :]))(q, r0, tt),
                        waits=[("s_v", pe)], inc=f"s_v2p{q}", amt=16)
                    hg_v2p0[vc // 4] = V("s_v2p0")
                    hg_v2p1[vc // 4] = V("s_v2p1")
            else:
                h = j - 32
                qkv_free[j] = E("vector", (lambda pj_, h_: lambda e: e.tensor_copy(
                    Q_ap(h_, 0, 1024), psAB[pj_][:, :]))(pj, h),
                    waits=[("s_t", qkv_mm[j])], inc="s_v")
            if j + 2 < 48:
                emit_qkv_cast(j + 2)
        for hg in range(4):
            E("gpsimd", (lambda hg_: lambda e: e.collective_compute(
                "AllGather", ALU.bypass,
                replica_groups=[[0, 1, 2, 3], [4, 5, 6, 7]],
                ins=[cc_in.ap()[hg_ * 1024:(hg_ + 1) * 1024, :].opt()],
                outs=[cc_out.ap()[hg_ * 4096:(hg_ + 1) * 4096, :].opt()]))(hg),
                waits=[("s_kst", hg_kst[hg]), ("s_v2p0", hg_v2p0[hg]),
                       ("s_v2p1", hg_v2p1[hg])],
                inc="s_cc", amt=1)
        QKV_LAST_EV = qkv_free[47]

        head_ol = {}
        head_ev = {}
        exp_c = {}
        sg_c = {}
        ol_c = {}
        kv_c = {}
        for h in range(NH):
            hp = h % 2
            wk = [("s_cc", h // 4 + 1)]
            if h >= 2:
                wk.append(("s_t", head_ol[h - 2]))
            for r in range(4):
                base = (h // 4) * 4096 + r * 1024 + (h % 4) * 128
                kv_c[h] = E("sync", (lambda hp_, r_, base_: lambda e: e.dma_start(
                    out=K_ap(hp_, r_ * 1024, (r_ + 1) * 1024),
                    in_=cc_out.ap()[base_:base_ + 128, :]))(hp, r, base),
                    waits=(wk if r == 0 else []), inc=f"s_kv{h % 2}", amt=16)
            for r in range(4):
                base = (h // 4) * 4096 + r * 1024 + 512 + (h % 4) * 128
                kv_c[h] = E("sync", (lambda hp_, r_, base_: lambda e: e.dma_start(
                    out=V_ap(hp_, r_ * 1024, (r_ + 1) * 1024).rearrange(
                        "p (m x) -> p m x", x=128),
                    in_=cc_out.ap()[base_:base_ + 128, :].rearrange(
                        "p (m x) -> p m x", x=128)))(hp, r, base),
                    inc=f"s_kv{h % 2}", amt=16)
            def emit_S(kc):
                gj = h * KC + kc
                pk = gj % 2
                wS = []
                if kc == 0:
                    wS.append((f"s_kv{h % 2}", kv_c[h]))
                if gj >= 2:
                    wS.append(("s_s", exp_c[gj - 2]))
                else:
                    wS.append(("s_v", QKV_LAST_EV))
                for hf in range(2):
                    sg_c[gj] = E("tensor", (lambda h_, hp_, kc_, pk_, hf_: lambda e: e.matmul(
                        psAB[pk_][:, hf_ * 512:(hf_ + 1) * 512],
                        lhsT=K_ap(hp_, kc_ * 128, (kc_ + 1) * 128),
                        rhs=Q_ap(h_, hf_ * 512, (hf_ + 1) * 512),
                        start=True, stop=True))(h, hp, kc, pk, hf),
                        waits=(wS if hf == 0 else []), inc="s_t")

            emit_S(0)
            for kc in range(KC):
                gj = h * KC + kc
                pk = gj % 2
                wE = [("s_t", sg_c[gj])]
                if gj >= 2:
                    wE.append(("s_t", ol_c[gj - 2]))
                exp_c[gj] = E("scalar", (lambda pk_: lambda e: e.activation(
                    out=abuf[pk_][:, :], in_=psAB[pk_][:, :], func=AF.Exp,
                    scale=SCALE, bias=expb_col[:, :]))(pk), waits=wE, inc="s_s")
                if kc + 1 < KC:
                    emit_S(kc + 1)
                wO = [("s_s", exp_c[gj])]
                if kc == 0 and h >= 1:
                    wO.append(("s_v", head_ev[h - 1]))
                for hf in range(2):
                    E("tensor", (lambda hp_, kc_, pk_, hf_: lambda e: e.matmul(
                        psC[:, hf_ * 512:(hf_ + 1) * 512],
                        lhsT=V_ap(hp_, kc_ * 128, (kc_ + 1) * 128),
                        rhs=abuf[pk_][:, hf_ * 512:(hf_ + 1) * 512],
                        start=(kc_ == 0), stop=(kc_ == KC - 1)))(hp, kc, pk, hf),
                        waits=(wO if hf == 0 else []))
                for hf in range(2):
                    ol_c[gj] = E("tensor", (lambda pk_, kc_, hf_: lambda e: e.matmul(
                        psD[:, hf_ * 512:(hf_ + 1) * 512],
                        lhsT=ones_bf[:, :],
                        rhs=abuf[pk_][:, hf_ * 512:(hf_ + 1) * 512],
                        start=(kc_ == 0), stop=(kc_ == KC - 1)))(pk, kc, hf),
                        inc="s_t")
            head_ol[h] = ol_c[h * KC + KC - 1]
            E("vector", lambda e: e.reciprocal(recip[:, :], psD[:, :]),
                waits=[("s_t", head_ol[h])])
            E("vector", lambda e: e.drain())
            head_ev[h] = E("vector", (lambda h_: lambda e: e.tensor_tensor(
                out=O_ap(h_, 0, 1024), in0=psC[:, :], in1=recip[:, :],
                op=ALU.mult))(h), inc="s_v")

        ATT_DONE_T = head_ol[NH - 1]
        ATT_DONE_V = head_ev[NH - 1]

        scl_tp = [None]

        def emit_cols(src_ap1d, n, dst, first_waits):
            nb = E("sync", (lambda n_: lambda e: e.dma_start(
                out=sgrows[0:n_, :],
                in_=src_ap1d.rearrange("(r c) -> r c", c=128)))(n),
                waits=list(first_waits) + (
                    [("s_t", scl_tp[0])] if scl_tp[0] is not None else []),
                inc="s_sg", amt=16)
            E("vector", (lambda n_: lambda e: e.tensor_copy(
                sgrows2[0:n_, :], sgrows[0:n_, :]))(n), waits=[("s_sg", nb)])
            cp = emit_spacer_inc()
            tp = E("tensor", (lambda n_: lambda e: e.transpose(
                psD[:, 512:512 + n_], sgrows2[0:n_, :], ident_f[0:n_, 0:n_]))(n),
                waits=[("s_v", cp)], inc="s_t")
            scl_tp[0] = tp
            return E("vector", (lambda n_, dst_: lambda e: e.tensor_copy(
                dst_[:, 0:n_], psD[:, 512:512 + n_]))(n, dst),
                waits=[("s_t", tp)], inc="s_v")

        bo_ev = emit_cols(bout_d, 16, boc,
                          [("s_t", ATT_DONE_T), ("s_v", ATT_DONE_V)])

        op_mm = {}
        op_cast = {}
        op_ev = {}
        op_tp = {}
        op_tp_ev = [bo_ev, bo_ev]
        last_sp = [None, None, None, None]
        op_dma = {}

        def emit_op_dma(oc):
            w = [("s_v", op_cast[oc - 2])] if oc >= 2 else [("s_t", ATT_DONE_T)]
            op_dma[oc] = E("gpsimd", (lambda oc_, po_: lambda e: e.dma_start(
                out=wstg[po_][:, :].rearrange("d (c o) -> d c o", o=128),
                in_=wout_d[oc_].rearrange("c d o -> d c o")))(oc, oc % 2),
                waits=w, inc=f"s_wg{oc % 2}", amt=16)

        def emit_op_cast(oc):
            wcst = [(f"s_wg{oc % 2}", op_dma[oc])]
            if oc >= 4:
                wcst.append(("s_t", op_mm[oc - 4]))
            else:
                wcst.append(("s_t", qkv_mm[47]))
            E("vector", (lambda po_, wb_: lambda e: e.tensor_copy(
                wsb[wb_][:, :], wstg[po_][:, :]))(oc % 2, oc % 4), waits=wcst)
            op_cast[oc] = emit_spacer_inc()

        emit_op_dma(0)
        emit_op_dma(1)
        emit_op_cast(0)
        emit_op_cast(1)
        op_last_pev = {}
        for oc in range(16):
            po = oc % 2
            wm = [("s_v", op_cast[oc]), ("s_v", bo_ev)]
            if oc >= 1:
                wm.append(("s_v", op_ev[oc - 1]))
            first = True
            for hc in range(16):
                for hf in range(2):
                    op_mm[oc] = E("tensor", (lambda po_, hc_, hf_, wb_: lambda e: e.matmul(
                        psAB[po_][:, hf_ * 512:(hf_ + 1) * 512],
                        lhsT=wsb[wb_][:, hc_ * 128:(hc_ + 1) * 128],
                        rhs=O_ap(hc_, hf_ * 512, (hf_ + 1) * 512),
                        start=(hc_ == 0), stop=(hc_ == 15)))(po, hc, hf, oc % 4),
                        waits=(wm if first else []), inc="s_t")
                    first = False
            if oc + 2 < 16:
                emit_op_dma(oc + 2)
                emit_op_cast(oc + 2)
            wxc = [("s_v", op_last_pev[oc - 1])] if oc >= 1 else \
                  [("s_t", ATT_DONE_T), ("s_v", ATT_DONE_V)]
            nxc = E("sync", (lambda oc_: lambda e: e.dma_start(
                out=recip[:, :].rearrange("p (tt o) -> p tt o", o=128),
                in_=x_d[:, oc_ * 128:(oc_ + 1) * 128].rearrange(
                    "(tt p) o -> p tt o", p=128)))(oc),
                waits=wxc, inc="s_xs0", amt=16)
            half = oc % 2
            wv = [("s_t", op_mm[oc])]
            if oc >= 2:
                wv.append(("s_t", op_tp[oc - 2]))
            op_ev[oc] = E("vector", (lambda oc_, po_, half_: lambda e: e.tensor_scalar(
                out=xn_tm[:, half_ * 1024:(half_ + 1) * 1024],
                in0=psAB[po_][:, :], scalar1=boc[:, oc_:oc_ + 1], scalar2=None,
                op0=ALU.add))(oc, po, half), waits=wv, inc="s_v")
            for tt in range(TT):
                q = tt % 2
                q4 = tt % 4
                reg = psCD[q][:, 256:384]
                wtp = [("s_v", op_ev[oc]), ("s_v", op_tp_ev[q])]
                op_tp[oc] = E("tensor", (lambda half_, tt_, reg_: lambda e: e.transpose(
                    reg_,
                    xn_tm[:, half_ * 1024 + tt_ * 128:half_ * 1024 + (tt_ + 1) * 128],
                    ident_f[:, :]))(half, tt, reg), waits=wtp, inc="s_t")
                wev = [("s_t", op_tp[oc]), ("s_xs0", nxc)]
                if last_sp[q4] is not None:
                    wev.append((f"s_sp{q4}", last_sp[q4]))
                ev = E("vector", (lambda q4_, tt_, reg_: lambda e: e.tensor_tensor(
                    out=x2p[q4_][:, :], in0=reg_,
                    in1=recip[:, tt_ * 128:(tt_ + 1) * 128],
                    op=ALU.add))(q4, tt, reg), waits=wev, inc="s_v")
                op_tp_ev[q] = ev
                op_last_pev[oc] = ev
                last_sp[q4] = E("sync", (lambda oc_, tt_, q4_: lambda e: e.dma_start(
                    out=x2sp.ap()[tt_ * 128:(tt_ + 1) * 128,
                                  oc_ * 128:(oc_ + 1) * 128],
                    in_=x2p[q4_][:, :]))(oc, tt, q4),
                    waits=[("s_v", ev)], inc=f"s_sp{q4}", amt=16)
        SPILL_W = [(f"s_sp{i}", V(f"s_sp{i}")) for i in range(4)]

        ln2_done = emit_ln(x2sp.ap(), SPILL_W,
                           (psC[:, 0:128], psD[:, 0:128]),
                           max(op_tp_ev))

        ev_sg = emit_cols(sg_d, HC, sgc, [("s_v", ln2_done)])
        ev_su = emit_cols(su_d, HC, suc, [("s_v", ev_sg)])
        ev_sd = emit_cols(sd_d, 16, sdc, [("s_v", ev_su)])

        g_mm = {}
        u_mm = {}
        silu_c = {}
        uev_c = {}
        pmul_c = {}
        cast_g = {}
        cast_u = {}
        for hc in range(HC):
            ph = hc % 2
            wwg = [("s_v", ev_sd)] if hc == 0 else [("s_v", cast_u[hc - 1])]
            ng = E("gpsimd", (lambda hc_: lambda e: e.dma_start(
                out=stg[0][:, :].rearrange("d (c o) -> d c o", o=128),
                in_=wg_d[hc_].rearrange("c d o -> d c o")))(hc),
                waits=wwg, inc="s_wg0", amt=16)
            nu = E("gpsimd", (lambda hc_: lambda e: e.dma_start(
                out=stg[1][:, :].rearrange("d (c o) -> d c o", o=128),
                in_=wu_d[hc_].rearrange("c d o -> d c o")))(hc),
                inc="s_wg1", amt=16)
            wcg = [("s_wg0", ng)]
            if hc >= 2:
                wcg.append(("s_t", g_mm[hc - 2]))
            else:
                wcg.append(("s_t", op_mm[15]))
            E("vector", (lambda hc_: lambda e: e.tensor_scalar(
                out=wsb[hc_ % 2][:, :], in0=stg[0][:, :], scalar1=128.0,
                scalar2=None, op0=ALU.subtract))(hc), waits=wcg)
            cast_g[hc] = emit_spacer_inc()
            wcu = [("s_wg1", nu)]
            if hc >= 2:
                wcu.append(("s_t", u_mm[hc - 2]))
            E("vector", (lambda hc_: lambda e: e.tensor_scalar(
                out=wsb[2 + hc_ % 2][:, :], in0=stg[1][:, :], scalar1=128.0,
                scalar2=None, op0=ALU.subtract))(hc), waits=wcu)
            cast_u[hc] = emit_spacer_inc()
            wgm = [("s_v", cast_g[hc])]
            if hc >= 2:
                wgm.append(("s_s", silu_c[hc - 2]))
            if hc < 2:
                wgm.append(("s_v", ln2_done))
            first = True
            for dc in range(DC):
                for hf in range(2):
                    g_mm[hc] = E("tensor", (lambda ph_, dc_, hf_, wb_=hc % 2: lambda e: e.matmul(
                        psAB[ph_][:, hf_ * 512:(hf_ + 1) * 512],
                        lhsT=wsb[wb_][:, dc_ * 128:(dc_ + 1) * 128],
                        rhs=xnf_ap(dc_, hf_ * 512, (hf_ + 1) * 512),
                        start=(dc_ == 0), stop=(dc_ == DC - 1)))(ph, dc, hf),
                        waits=(wgm if first else []), inc="s_t")
                    first = False
            wsi = [("s_t", g_mm[hc]), ("s_v", ev_sg)]
            if hc >= 1:
                wsi.append(("s_v", pmul_c[hc - 1]))
            silu_c[hc] = E("scalar", (lambda ph_, hc_: lambda e: e.activation(
                out=sgt[:, :], in_=psAB[ph_][:, :], func=AF.Silu,
                scale=sgc[:, hc_:hc_ + 1]))(ph, hc), waits=wsi, inc="s_s")
            wum = [("s_v", cast_u[hc])]
            if hc >= 2:
                wum.append(("s_v", uev_c[hc - 2]))
            if hc == 0:
                wum.append(("s_v", ln2_done))
            elif hc == 1:
                wum.append(("s_v", ev_sd))
            first = True
            for dc in range(DC):
                for hf in range(2):
                    u_mm[hc] = E("tensor", (lambda ph_, dc_, hf_, wb_=2 + hc % 2: lambda e: e.matmul(
                        psCD[ph_][:, hf_ * 512:(hf_ + 1) * 512],
                        lhsT=wsb[wb_][:, dc_ * 128:(dc_ + 1) * 128],
                        rhs=xnf_ap(dc_, hf_ * 512, (hf_ + 1) * 512),
                        start=(dc_ == 0), stop=(dc_ == DC - 1)))(ph, dc, hf),
                        waits=(wum if first else []), inc="s_t")
                    first = False
            uev_c[hc] = E("vector", (lambda ph_, hc_: lambda e: e.tensor_scalar(
                out=utmp[:, :], in0=psCD[ph_][:, :], scalar1=suc[:, hc_:hc_ + 1],
                scalar2=None, op0=ALU.mult))(ph, hc),
                waits=[("s_t", u_mm[hc]), ("s_v", ev_su)], inc="s_v")
            pmul_c[hc] = E("vector", (lambda hc_: lambda e: e.tensor_tensor(
                out=P_ap(hc_, 0, 1024), in0=sgt[:, :], in1=utmp[:, :],
                op=ALU.mult))(hc), waits=[("s_s", silu_c[hc])], inc="s_v")
        FFN_GU_V = pmul_c[HC - 1]
        FFN_GU_T = max(g_mm[HC - 1], u_mm[HC - 1])

        dn_piece_mm = {}
        dn_piece_cast = {}
        dn_last_pev = {}
        dn_ev = {}
        dn_tp = {}
        dn_tp_ev = [None, None]
        last_out = [None, None, None, None]
        for oc in range(16):
            po = oc % 2
            for pc in range(4):
                idx = oc * 4 + pc
                sp = idx % 2
                sw = idx % 4
                wwd = []
                if idx >= 2:
                    wwd.append(("s_v", dn_piece_cast[idx - 2]))
                else:
                    wwd.append(("s_v", FFN_GU_V))
                nd = E("gpsimd", (lambda oc_, pc_, sp_: lambda e: e.dma_start(
                    out=stg[sp_][:, 0:1408].rearrange("d (c o) -> d c o", o=128),
                    in_=wd_d[oc_, pc_ * 11:(pc_ + 1) * 11].rearrange(
                        "c d o -> d c o")))(oc, pc, sp),
                    waits=wwd, inc=f"s_wg{sp}", amt=16)
                wcd = [(f"s_wg{sp}", nd)]
                if idx >= 4:
                    wcd.append(("s_t", dn_piece_mm[idx - 4]))
                else:
                    wcd.append(("s_t", FFN_GU_T))
                E("vector", (lambda sp_, sw_: lambda e: e.tensor_scalar(
                    out=wsb[sw_][:, 0:1408], in0=stg[sp_][:, 0:1408],
                    scalar1=128.0, scalar2=None, op0=ALU.subtract))(sp, sw),
                    waits=wcd)
                dn_piece_cast[idx] = emit_spacer_inc()
                first = True
                for ti in range(11):
                    hcg = pc * 11 + ti
                    wmm = []
                    if first:
                        wmm.append(("s_v", dn_piece_cast[idx]))
                        if pc == 0:
                            if oc >= 1:
                                wmm.append(("s_v", dn_ev[oc - 1]))
                            else:
                                wmm.append(("s_v", FFN_GU_V))
                    for hf in range(2):
                        dn_piece_mm[idx] = E("tensor",
                            (lambda po_, sw_, ti_, hcg_, hf_: lambda e: e.matmul(
                                psAB[po_][:, hf_ * 512:(hf_ + 1) * 512],
                                lhsT=wsb[sw_][:, ti_ * 128:(ti_ + 1) * 128],
                                rhs=P_ap(hcg_, hf_ * 512, (hf_ + 1) * 512),
                                start=(hcg_ == 0), stop=(hcg_ == HC - 1)))(
                                    po, sw, ti, hcg, hf),
                            waits=(wmm if hf == 0 and wmm else []), inc="s_t")
                        first = False
            dn_mm_last = dn_piece_mm[oc * 4 + 3]
            wxc = [("s_v", dn_last_pev[oc - 1])] if oc >= 1 else \
                  [("s_v", FFN_GU_V)]
            nxc = E("sync", (lambda oc_: lambda e: e.dma_start(
                out=recip[:, :].rearrange("p (tt o) -> p tt o", o=128),
                in_=x2sp.ap()[:, oc_ * 128:(oc_ + 1) * 128].rearrange(
                    "(tt p) o -> p tt o", p=128)))(oc),
                waits=wxc, inc="s_xs1", amt=16)
            half = oc % 2
            wv = [("s_t", dn_mm_last), ("s_v", ev_sd)]
            if oc >= 2:
                wv.append(("s_t", dn_tp[oc - 2]))
            dn_ev[oc] = E("vector", (lambda oc_, po_, half_: lambda e: e.tensor_scalar(
                out=xn_tm[:, half_ * 1024:(half_ + 1) * 1024],
                in0=psAB[po_][:, :], scalar1=sdc[:, oc_:oc_ + 1], scalar2=None,
                op0=ALU.mult))(oc, po, half), waits=wv, inc="s_v")
            for tt in range(TT):
                q = tt % 2
                q4 = tt % 4
                reg = psCD[q][:, 384:512]
                wtp = [("s_v", dn_ev[oc])]
                if dn_tp_ev[q] is not None:
                    wtp.append(("s_v", dn_tp_ev[q]))
                dn_tp[oc] = E("tensor", (lambda half_, tt_, reg_: lambda e: e.transpose(
                    reg_,
                    xn_tm[:, half_ * 1024 + tt_ * 128:half_ * 1024 + (tt_ + 1) * 128],
                    ident_f[:, :]))(half, tt, reg), waits=wtp, inc="s_t")
                wev = [("s_t", dn_tp[oc]), ("s_xs1", nxc)]
                if last_out[q4] is not None:
                    wev.append((f"s_out{q4}", last_out[q4]))
                ev = E("vector", (lambda q4_, tt_, reg_: lambda e: e.tensor_tensor(
                    out=x2p[q4_][:, :], in0=reg_,
                    in1=recip[:, tt_ * 128:(tt_ + 1) * 128],
                    op=ALU.add))(q4, tt, reg), waits=wev, inc="s_v")
                dn_tp_ev[q] = ev
                dn_last_pev[oc] = ev
                last_out[q4] = E("sync", (lambda oc_, tt_, q4_: lambda e: e.dma_start(
                    out=out_d[tt_ * 128:(tt_ + 1) * 128,
                              oc_ * 128:(oc_ + 1) * 128],
                    in_=x2p[q4_][:, :]))(oc, tt, q4),
                    waits=[("s_v", ev)], inc=f"s_out{q4}", amt=16)

        for i in range(4):
            E("sync", (lambda i_: lambda e: e.wait_ge(
                sems[f"s_out{i_}"], V(f"s_out{i_}")))(i))

        def emit_engine(eng_obj, name):
            for waits, fn, inc, amt in ops[name]:
                for (s, v) in waits:
                    eng_obj.wait_ge(sems[s], v)
                instr = fn(eng_obj)
                if inc is None:
                    continue
                if name in ("vector", "scalar") or (name == "gpsimd" and inc == "s_p"):
                    eng_obj.drain().then_inc(sems[inc], amt)
                elif instr is not None:
                    instr.then_inc(sems[inc], amt)

        @block.sync
        def _(sync):
            emit_engine(sync, "sync")

        @block.gpsimd
        def _(gpsimd):
            emit_engine(gpsimd, "gpsimd")

        @block.tensor
        def _(tensor):
            emit_engine(tensor, "tensor")

        @block.scalar
        def _(scalar):
            emit_engine(scalar, "scalar")

        @block.vector
        def _(vector):
            emit_engine(vector, "vector")

    return nc


vtp_last = {}


def kernel(L0_attn_hidden_states, L0_attn_w_qkv, L0_attn_b_qkv, L0_attn_w_out,
           L0_attn_b_out, L0_ffn_wq_gate, L0_ffn_s_gate, L0_ffn_wq_up,
           L0_ffn_s_up, L0_ffn_wq_down, L0_ffn_s_down):
    global vtp_last
    vtp_last = {}

    x = np.ascontiguousarray(np.asarray(L0_attn_hidden_states, dtype=np.float32))
    w_qkv = np.asarray(L0_attn_w_qkv, dtype=np.float32)
    b_qkv = np.asarray(L0_attn_b_qkv, dtype=np.float32)
    w_out = np.asarray(L0_attn_w_out, dtype=np.float32)
    b_out = np.ascontiguousarray(np.asarray(L0_attn_b_out, dtype=np.float32))
    wq_g = np.asarray(L0_ffn_wq_gate, dtype=np.int32)
    wq_u = np.asarray(L0_ffn_wq_up, dtype=np.int32)
    wq_d = np.asarray(L0_ffn_wq_down, dtype=np.int32)
    s_g = np.ascontiguousarray(np.asarray(L0_ffn_s_gate, dtype=np.float32))
    s_u = np.ascontiguousarray(np.asarray(L0_ffn_s_up, dtype=np.float32))
    s_d = np.ascontiguousarray(np.asarray(L0_ffn_s_down, dtype=np.float32))

    WT = np.ascontiguousarray(w_qkv.T)
    blocks = []
    for lo in (2048, 4096, 0):
        blocks.append(WT[:, lo:lo + 2048].reshape(DC, 128, 16, 128)
                      .transpose(2, 0, 1, 3))
    wqkv_b = np.ascontiguousarray(np.concatenate(blocks, axis=0))
    bqkvr = np.ascontiguousarray(
        np.concatenate([b_qkv[2048:4096], b_qkv[4096:6144], b_qkv[0:2048]]))

    wout_b = np.ascontiguousarray(
        np.ascontiguousarray(w_out.T).reshape(16, 128, 16, 128)
        .transpose(2, 0, 1, 3))
    wg_b = np.ascontiguousarray(
        np.ascontiguousarray(wq_g.T).reshape(DC, 128, HC, 128)
        .transpose(2, 0, 1, 3))
    wu_b = np.ascontiguousarray(
        np.ascontiguousarray(wq_u.T).reshape(DC, 128, HC, 128)
        .transpose(2, 0, 1, 3))
    wd_b = np.ascontiguousarray(
        np.ascontiguousarray(wq_d.T).reshape(HC, 128, 16, 128)
        .transpose(2, 0, 1, 3))

    nc = build_nc()
    in_maps = []
    for c in range(NCORE):
        bb = c // 4
        t0 = (c % 4) * T
        in_maps.append({
            "x": np.ascontiguousarray(x[bb, t0:t0 + T, :]),
            "wqkv": wqkv_b,
            "wo": wout_b,
            "bqkvr": bqkvr,
            "bo": b_out,
            "wg": wg_b,
            "wu": wu_b,
            "wd": wd_b,
            "sg": s_g,
            "su": s_u,
            "sd": s_d,
        })
    res = run_bass_kernel_spmd(nc, in_maps, core_ids=list(range(NCORE)))
    out = np.empty((B, S, D), dtype=np.float32)
    for c in range(NCORE):
        bb = c // 4
        t0 = (c % 4) * T
        out[bb, t0:t0 + T, :] = res.results[c]["out"]
    return out
